# version 01
# baseline (speedup 1.0000x reference)
import numpy as np
from contextlib import ExitStack

import concourse.bass as bass
import concourse.mybir as mybir
from concourse.bass_utils import run_bass_kernel_spmd

dt = mybir.dt
AF = mybir.ActivationFunctionType
ALU = mybir.AluOpType
AX = mybir.AxisListType

F32 = dt.float32
BF16 = dt.bfloat16
I32 = dt.int32

B, S, HID = 4, 2048, 4096
H, NOPE, ROPE, VD, QKD = 32, 128, 64, 128, 192
QR, KVR = 1536, 512
IH, IDD, TOPK = 16, 128, 1024
II = 11008
EPS = 1e-5
THETA = 10000.0
NQ = 1024
NQB = 8
NKB = 16
MAGIC = 12582912.0
I2PI = 1.0 / (2.0 * np.pi)
TWO_PI = 2.0 * np.pi

KCS = [1, 1, 2, 2, 3, 3, 4, 4]
ACC_OFF = [0]
for _k in KCS:
    ACC_OFF.append(ACC_OFF[-1] + _k * 512)
ACC_W = ACC_OFF[-1]
KB_PER_QC = [8, 16]

BISECT_ITERS = 22
BIG_NEG = -1.0e9
THR_LO, THR_HI = -64.0, 64.0


class Sem:
    def __init__(self, handle):
        self.h = handle
        self.n = 0


def feat_proj(nc, sd_w, s_pe, s_ev, w_tensor, w_rowlen, ft_list, wbufs, xT,
              xT_stride, tok0, ntok, psums, evac_fn, n_dblk=32):
    nft = len(ft_list)
    np_ = len(psums)
    dma0 = sd_w.n
    pe0 = s_pe.n
    ev0 = s_ev.n
    nchunks = ntok // 512

    with nc.Block() as blk:
        @blk.gpsimd
        def _(g):
            for ft, (coff, w) in enumerate(ft_list):
                if ft >= 2:
                    g.wait_ge(s_pe.h, pe0 + ft - 1)
                src = bass.AP(w_tensor, coff,
                              [[w_rowlen, 128], [w_rowlen * 128, n_dblk], [1, w]])
                g.dma_start(wbufs[ft % 2][:, 0:n_dblk * w], src
                            ).then_inc(sd_w.h, 16)
        sd_w.n = dma0 + 16 * nft

        @blk.tensor
        def _(t):
            for ft, (coff, w) in enumerate(ft_list):
                wb = wbufs[ft % 2]
                t.wait_ge(sd_w.h, dma0 + 16 * (ft + 1))
                if ft >= np_:
                    t.wait_ge(s_ev.h, ev0 + ft - np_ + 1)
                ps = psums[ft % np_]
                mm = None
                for db in range(n_dblk):
                    lhs = wb[:, db * w:(db + 1) * w]
                    for ck in range(nchunks):
                        mm = t.matmul(ps[0:w, ck * 512:(ck + 1) * 512], lhs,
                                      xT[:, db * xT_stride + ck * 512:
                                         db * xT_stride + (ck + 1) * 512],
                                      start=(db == 0), stop=(db == n_dblk - 1))
                mm.then_inc(s_pe.h, 1)
        s_pe.n = pe0 + nft

        @blk.scalar
        def _(sc):
            for ft in range(nft):
                sc.wait_ge(s_pe.h, pe0 + ft + 1)
                last = evac_fn(ft, psums[ft % np_], sc, tok0)
                last.then_inc(s_ev.h, 1)
        s_ev.n = ev0 + nft


def build_rope_tables(nc, sd_x, pos_src, width, cos_dst, sin_dst, invf,
                      ang, rty):
    if True:
        with nc.Block() as b2:
            @b2.vector
            def _(v):
                v.tensor_scalar(ang[:], pos_src, invf[:, 0:1], None, ALU.mult)
        with nc.Block() as b2:
            @b2.vector
            def _(v):
                v.tensor_scalar(rty[:], ang[:], I2PI, None, ALU.mult)
        with nc.Block() as b2:
            @b2.vector
            def _(v):
                v.tensor_scalar(sin_dst, rty[:], MAGIC, MAGIC,
                                ALU.add, ALU.subtract)
        with nc.Block() as b2:
            @b2.vector
            def _(v):
                v.tensor_sub(sin_dst, rty[:], sin_dst)
                v.tensor_scalar(rty[:], ang[:], I2PI, 0.25, ALU.mult, ALU.add)
        with nc.Block() as b2:
            @b2.vector
            def _(v):
                v.tensor_scalar(cos_dst, rty[:], MAGIC, MAGIC,
                                ALU.add, ALU.subtract)
        with nc.Block() as b2:
            @b2.vector
            def _(v):
                v.tensor_sub(cos_dst, rty[:], cos_dst)
        with nc.Block() as b2:
            @b2.scalar
            def _(sc):
                sc.activation(sin_dst, sin_dst, AF.Sin, scale=TWO_PI)
                sc.activation(cos_dst, cos_dst, AF.Sin, scale=TWO_PI)


LAST_SEMS = {}


def build(nc: bass.Bass, taps=(), upto=99, p6_heads=H, p6_stages=3, p6_sub=9, p6_setup=9):
    taps = set(taps)

    def din(name, shape, dtype=F32):
        return nc.dram_tensor(name, shape, dtype, kind="ExternalInput").ap()

    x_full = din("x_full", [S, HID])
    x_q = din("x_q", [NQ, HID])
    qpos_pb = din("qpos_pb", [128, NQB])
    qpos_row = din("qpos_row", [1, NQ])
    q_a_w = din("q_a_w", [HID, QR])
    q_a_norm = din("q_a_norm", [QR])
    q_b_w = din("q_b_w", [QR, H * QKD])
    kv_a_w = din("kv_a_w", [HID, KVR + ROPE])
    kv_a_norm = din("kv_a_norm", [KVR])
    kv_b_w = din("kv_b_w", [KVR, H * (NOPE + VD)])
    o_w = din("o_w", [H * VD, HID])
    idx_wq = din("idx_wq", [HID, IH * IDD])
    idx_wk = din("idx_wk", [HID, IDD])
    idx_ww = din("idx_ww", [HID, IH])
    gate_wT = din("gate_wT", [HID, II], I32)
    up_wT = din("up_wT", [HID, II], I32)
    down_wT = din("down_wT", [II, HID], I32)
    gate_s = din("gate_s", [1])
    up_s = din("up_s", [1])
    down_s = din("down_s", [1])

    out_loc = nc.dram_tensor("out_loc", [NQ, HID], F32, kind="ExternalOutput").ap()

    xbf = nc.dram_tensor("xbf", [S, HID], BF16).ap()
    xqbf = nc.dram_tensor("xqbf", [NQ, HID], BF16).ap()
    attn_pre = nc.dram_tensor("attn_pre", [NQ, HID], F32).ap()
    x1f = nc.dram_tensor("x1f", [NQ, HID], F32).ap()
    x1bf = nc.dram_tensor("x1bf", [NQ, HID], BF16).ap()
    hTd = nc.dram_tensor("hTd", [II, NQ], BF16).ap()
    ffn_pre = nc.dram_tensor("ffn_pre", [NQ, HID], F32).ap()

    def tap(name, shape, dtype=F32):
        if name in taps:
            return nc.dram_tensor("tap_" + name, shape, dtype,
                                  kind="ExternalOutput").ap()
        return None

    t_kvnT = tap("kvnT", [128, 4 * S])
    t_kpeT = tap("kpeT", [64, S])
    t_ikT = tap("ikT", [128, S])
    t_qlatT = tap("qlatT", [128, 12 * NQ])
    t_iqT = tap("iqT", [128, IH * NQ])
    t_iwT = tap("iwT", [128, NQB * IH])
    t_cosk = tap("cosk", [32, S])
    t_sink = tap("sink", [32, S])
    t_acc = tap("acc", [128, ACC_W])
    t_thr = tap("thr", [128, NQB])
    t_maskT = tap("maskT", [128, NKB * NQ])
    t_attnoT = tap("attnoT", [128, H * NQ], BF16)
    t_x1 = tap("x1", [NQ, HID])
    t_hT = tap("hT", [II, NQ], BF16)

    with ExitStack() as stk:
        ec = stk.enter_context

        def sb(name, shape, dtype, side="right"):
            return ec(nc.sbuf_tensor(name, shape, dtype, side=side))

        ident = sb("ident", [128, 128], BF16, side="left")
        identf = sb("identf", [128, 128], F32, side="left")
        ones_f = sb("ones_f", [128, 1], F32, side="left")
        ones_row = sb("ones_row", [1, 128], F32, side="left")
        scl = sb("scl", [128, 4], F32, side="left")
        qpos_sb = sb("qpos_sb", [128, NQB], F32, side="left")
        qposh_sb = sb("qposh_sb", [128, NQB], F32, side="left")
        qanorm = sb("qanorm", [128, 12], F32, side="left")
        kvnorm = sb("kvnorm", [128, 4], F32, side="left")
        epsb = sb("epsb", [128, 1], F32, side="left")
        invf = sb("invf", [32, 1], F32, side="left")

        _sd_w = ec(nc.semaphore("sd_w"))
        _sd_g = ec(nc.semaphore("sd_g"))
        _sd_x = ec(nc.semaphore("sd_x"))
        _sd_o = ec(nc.semaphore("sd_o"))
        _s_pe = ec(nc.semaphore("s_pe"))
        _s_ev = ec(nc.semaphore("s_ev"))
        _s_v2 = ec(nc.semaphore("s_v2"))
        _s_gp = ec(nc.semaphore("s_gp"))
        sd_w, sd_x, sd_o = Sem(_sd_w), Sem(_sd_x), Sem(_sd_o)
        sd_g = Sem(_sd_g)
        s_pe, s_ev, s_v2, s_gp = Sem(_s_pe), Sem(_s_ev), Sem(_s_v2), Sem(_s_gp)
        LAST_SEMS.clear()
        LAST_SEMS.update(dict(sd_w=sd_w, sd_x=sd_x, sd_o=sd_o, sd_g=sd_g,
                              s_pe=s_pe, s_ev=s_ev, s_v2=s_v2, s_gp=s_gp))

        with nc.Block() as blk:
            @blk.gpsimd
            def _(g):
                g.memset(ident[:], 0.0)
                g.affine_select(out=ident[:], in_=ident[:],
                                compare_op=ALU.not_equal, fill=1.0,
                                base=0, pattern=[[-1, 128]], channel_multiplier=1)
                g.memset(identf[:], 0.0)
                g.affine_select(out=identf[:], in_=identf[:],
                                compare_op=ALU.not_equal, fill=1.0,
                                base=0, pattern=[[-1, 128]], channel_multiplier=1)
                g.memset(ones_f[:], 1.0)
                g.memset(ones_row[:], 1.0)
                g.memset(epsb[:], EPS)
                g.iota(invf[:], [[0, 1]], base=0, channel_multiplier=1,
                       allow_small_or_imprecise_dtypes=True)
                n = 0
                g.dma_start(qpos_sb[:], qpos_pb[:]).then_inc(_sd_g, 16); n += 1
                g.dma_start(scl[:, 0:1],
                            gate_s[None, :].broadcast_to([128, 1])
                            ).then_inc(_sd_g, 16); n += 1
                g.dma_start(scl[:, 1:2],
                            up_s[None, :].broadcast_to([128, 1])
                            ).then_inc(_sd_g, 16); n += 1
                g.dma_start(scl[:, 2:3],
                            down_s[None, :].broadcast_to([128, 1])
                            ).then_inc(_sd_g, 16); n += 1
                for j in range(12):
                    g.dma_start(qanorm[:, j:j + 1],
                                q_a_norm[j * 128:(j + 1) * 128][:, None]
                                ).then_inc(_sd_g, 16); n += 1
                for j in range(4):
                    g.dma_start(kvnorm[:, j:j + 1],
                                kv_a_norm[j * 128:(j + 1) * 128][:, None]
                                ).then_inc(_sd_g, 16); n += 1
                sd_g.n += 16 * n
                g.wait_ge(_sd_g, sd_g.n)

        with nc.Block() as blk:
            @blk.vector
            def _(v):
                for qb in range(NQB):
                    base = (KCS[qb] - 1) * 512
                    v.tensor_scalar(qposh_sb[:, qb:qb + 1], qpos_sb[:, qb:qb + 1],
                                    -float(base), None, ALU.add)

            @blk.scalar
            def _(sc):
                sc.activation(invf[:], invf[:], AF.Exp,
                              scale=-float(np.log(THETA)) / 32.0)

        with nc.Block() as blk:
            @blk.gpsimd
            def _(g):
                for i in range(4):
                    g.dma_start(xbf[i * 512:(i + 1) * 512, :],
                                x_full[i * 512:(i + 1) * 512, :]).then_inc(_sd_g, 16)
                for i in range(2):
                    g.dma_start(xqbf[i * 512:(i + 1) * 512, :],
                                x_q[i * 512:(i + 1) * 512, :]).then_inc(_sd_g, 16)
                sd_g.n += 16 * 6
                g.wait_ge(_sd_g, sd_g.n)

        if upto < 1:
            _finish_stub(nc, out_loc, _sd_g, sd_g)
            return nc

        with ExitStack() as stkA:
            ecA = stkA.enter_context

            def sbA(name, shape, dtype):
                return ecA(nc.sbuf_tensor(name, shape, dtype, side="right"))

            kvnT = sbA("kvnT", [128, 4 * S], BF16)
            ikT = sbA("ikT", [128, S], BF16)
            k_peT = sbA("k_peT", [64, S], BF16)
            iw_t = sbA("iw_t", [128, NQB * IH], F32)
            q_latT = sbA("q_latT", [128, 12 * NQ], BF16)

            with (
                nc.sbuf_tensor("cosk", [32, S], F32, side="right") as cosk,
                nc.sbuf_tensor("sink", [32, S], F32, side="right") as sink,
                nc.sbuf_tensor("xTh", [128, 32 * 1024], BF16, side="right") as xTh,
                nc.sbuf_tensor("wbuf0", [128, 32 * 128], BF16, side="right") as wbuf0,
                nc.sbuf_tensor("wbuf1", [128, 32 * 128], BF16, side="right") as wbuf1,
                nc.sbuf_tensor("sqb", [128, S], F32, side="right") as sqb,
                nc.sbuf_tensor("kpe0", [32, S], F32, side="right") as kpe0,
                nc.sbuf_tensor("kpe1", [32, S], F32, side="right") as kpe1,
                nc.sbuf_tensor("rsc", [32, S], F32, side="right") as rsc,
                nc.psum_tensor("ps0", [128, 1024], F32) as ps0,
                nc.psum_tensor("ps1", [128, 1024], F32) as ps1,
                nc.psum_tensor("psb", [128, S], F32) as psb,
            ):
                with nc.Block() as blk:
                    @blk.gpsimd
                    def _(g):
                        g.iota(sqb[0:32, 0:S], [[1, S]], base=0,
                               channel_multiplier=0,
                               allow_small_or_imprecise_dtypes=True)
                build_rope_tables(nc, sd_x, sqb[0:32, 0:S], S, cosk[:], sink[:],
                                  invf, sqb[0:32, 0:S], kpe0[:, 0:S])
                if t_cosk is not None:
                    with nc.Block() as blk:
                        @blk.gpsimd
                        def _(sy):
                            sy.dma_start(t_cosk[:], cosk[:]).then_inc(_sd_g, 16)
                            sy.dma_start(t_sink[:], sink[:]).then_inc(_sd_g, 16)
                            sd_g.n += 32
                            sy.wait_ge(_sd_g, sd_g.n)

                wbufs = [wbuf0, wbuf1]
                ftl_ckv = [(0, 128), (128, 128), (256, 128), (384, 128), (512, 64)]

                if upto < 2:
                    _finish_stub(nc, out_loc, _sd_g, sd_g)
                    return nc

                for hf in range(2):
                    with nc.Block() as blk:
                        @blk.sync
                        def _(sy):
                            for db in range(32):
                                sy.dma_start(
                                    xTh[:, db * 1024:(db + 1) * 1024],
                                    xbf[hf * 1024:(hf + 1) * 1024,
                                        db * 128:(db + 1) * 128],
                                    transpose=True).then_inc(_sd_x, 16)
                            sd_x.n += 16 * 32
                            sy.wait_ge(_sd_x, sd_x.n)

                    def evac_ckv(ft, ps, sc, tok0, hf=hf):
                        if ft < 4:
                            return sc.activation(
                                kvnT[:, ft * S + hf * 1024:
                                     ft * S + (hf + 1) * 1024],
                                ps[:, 0:1024], AF.Copy)
                        sc.activation(kpe0[:, hf * 1024:(hf + 1) * 1024],
                                      ps[0:32, 0:1024], AF.Copy)
                        return sc.activation(kpe1[:, hf * 1024:(hf + 1) * 1024],
                                             ps[32:64, 0:1024], AF.Copy)

                    feat_proj(nc, sd_w, s_pe, s_ev, kv_a_w.tensor, KVR + ROPE,
                              ftl_ckv, wbufs, xTh, 1024, hf * 1024, 1024,
                              [ps0, ps1], evac_ckv)

                    def evac_ik(ft, ps, sc, tok0, hf=hf):
                        return sc.activation(ikT[:, hf * 1024:(hf + 1) * 1024],
                                             ps[:, 0:1024], AF.Copy)

                    feat_proj(nc, sd_w, s_pe, s_ev, idx_wk.tensor, IDD,
                              [(0, 128)], wbufs, xTh, 1024, hf * 1024, 1024,
                              [ps0], evac_ik)

                if upto == 12:
                    with nc.Block() as blk:
                        @blk.gpsimd
                        def _(sy):
                            if t_kvnT is not None:
                                sy.dma_start(t_kvnT[:], kvnT[:]).then_inc(_sd_g, 16)
                                sy.dma_start(t_kpeT[0:32, :], kpe0[:]).then_inc(_sd_g, 16)
                                sy.dma_start(t_kpeT[32:64, :], kpe1[:]).then_inc(_sd_g, 16)
                                sd_g.n += 48
                                sy.wait_ge(_sd_g, sd_g.n)
                    _finish_stub(nc, out_loc, _sd_g, sd_g)
                    return nc

                with nc.Block() as blk:
                    ev0 = s_ev.n
                    pe0 = s_pe.n

                    @blk.scalar
                    def _(sc):
                        for ft in range(4):
                            sc.activation(sqb[:], kvnT[:, ft * S:(ft + 1) * S],
                                          AF.Square).then_inc(_s_ev, 1)
                            if ft < 3:
                                sc.wait_ge(_s_pe, pe0 + ft + 1)
                    s_ev.n = ev0 + 4

                    @blk.tensor
                    def _(t):
                        for ft in range(4):
                            t.wait_ge(_s_ev, ev0 + ft + 1)
                            for ck in range(4):
                                mm = t.matmul(psb[0:1, ck * 512:(ck + 1) * 512],
                                              ones_f[:],
                                              sqb[:, ck * 512:(ck + 1) * 512],
                                              start=(ft == 0), stop=(ft == 3))
                            mm.then_inc(_s_pe, 1)
                    s_pe.n = pe0 + 4

                with nc.Block() as blk:
                    @blk.scalar
                    def _(sc):
                        sc.activation(sqb[0:1, 0:S], psb[0:1, 0:S], AF.Ln,
                                      bias=epsb[0:1, 0:1], scale=1.0 / KVR)
                with nc.Block() as blk:
                    @blk.scalar
                    def _(sc):
                        sc.activation(sqb[0:1, 0:S], sqb[0:1, 0:S], AF.Exp,
                                      scale=-0.5)
                with nc.Block() as blk:
                    @blk.tensor
                    def _(t):
                        for ck in range(4):
                            t.matmul(psb[:, ck * 512:(ck + 1) * 512], ones_row[:],
                                     sqb[0:1, ck * 512:(ck + 1) * 512],
                                     start=True, stop=True)
                with nc.Block() as blk:
                    @blk.vector
                    def _(v):
                        for ft in range(4):
                            v.scalar_tensor_tensor(kvnT[:, ft * S:(ft + 1) * S],
                                                   kvnT[:, ft * S:(ft + 1) * S],
                                                   kvnorm[:, ft:ft + 1],
                                                   psb[:, 0:S], ALU.mult, ALU.mult)

                if upto == 13:
                    with nc.Block() as blk:
                        @blk.gpsimd
                        def _(sy):
                            if t_kvnT is not None:
                                sy.dma_start(t_kvnT[:], kvnT[:]).then_inc(_sd_g, 16)
                                sd_g.n += 16
                                sy.wait_ge(_sd_g, sd_g.n)
                    _finish_stub(nc, out_loc, _sd_g, sd_g)
                    return nc

                with nc.Block() as blk:
                    @blk.vector
                    def _(v):
                        v.tensor_mul(sqb[0:32, 0:S], kpe1[:, 0:S], sink[:])
                        v.tensor_mul(rsc[:, 0:S], kpe0[:, 0:S], cosk[:])
                with nc.Block() as blk:
                    @blk.vector
                    def _(v):
                        v.tensor_sub(k_peT[0:32, 0:S], rsc[:, 0:S],
                                     sqb[0:32, 0:S])
                        v.tensor_mul(rsc[:, 0:S], kpe0[:, 0:S], sink[:])
                        v.tensor_mul(sqb[0:32, 0:S], kpe1[:, 0:S], cosk[:])
                with nc.Block() as blk:
                    @blk.vector
                    def _(v):
                        v.tensor_add(rsc[:, 0:S], rsc[:, 0:S], sqb[0:32, 0:S])
                with nc.Block() as blk:
                    @blk.scalar
                    def _(sc):
                        sc.activation(k_peT[32:64, 0:S], rsc[:, 0:S], AF.Copy)

                if t_kvnT is not None:
                    with nc.Block() as blk:
                        @blk.gpsimd
                        def _(sy):
                            sy.dma_start(t_kvnT[:], kvnT[:]).then_inc(_sd_g, 16)
                            sy.dma_start(t_kpeT[:], k_peT[:]).then_inc(_sd_g, 16)
                            sy.dma_start(t_ikT[:], ikT[:]).then_inc(_sd_g, 16)
                            sd_g.n += 48
                            sy.wait_ge(_sd_g, sd_g.n)

            if upto < 3:
                _finish_stub(nc, out_loc, _sd_g, sd_g)
                return nc

            with ExitStack() as stkQ:
                maskT = stkQ.enter_context(
                    nc.sbuf_tensor("maskT", [128, NKB * NQ], BF16, side="right"))
                with nc.Block() as blk:
                    @blk.gpsimd
                    def _(g):
                        g.memset(maskT[:], 0.0)
                stkI = stkQ.enter_context(ExitStack())
                iqT = stkI.enter_context(
                    nc.sbuf_tensor("iqT", [128, IH * NQ], BF16, side="right"))

                with (
                    nc.sbuf_tensor("xTq", [128, 32 * NQ], BF16,
                                   side="right") as xTq,
                    nc.sbuf_tensor("wbufa", [128, 32 * 128], BF16,
                                   side="right") as wbufa,
                    nc.sbuf_tensor("wbufb", [128, 32 * 128], BF16,
                                   side="right") as wbufb,
                    nc.sbuf_tensor("sqq", [128, NQ], F32, side="right") as sqq,
                    nc.sbuf_tensor("iwst", [16, NQ], BF16, side="right") as iwst,
                    nc.psum_tensor("psq0", [128, NQ], F32) as psq0,
                    nc.psum_tensor("psq1", [128, NQ], F32) as psq1,
                    nc.psum_tensor("psqb", [128, NQ], F32) as psqb,
                    nc.psum_tensor("psqt", [128, 16], BF16) as psqt,
                ):
                    with nc.Block() as blk:
                        @blk.sync
                        def _(sy):
                            for db in range(32):
                                sy.dma_start(xTq[:, db * NQ:(db + 1) * NQ],
                                             xqbf[:, db * 128:(db + 1) * 128],
                                             transpose=True).then_inc(_sd_x, 16)
                            sd_x.n += 16 * 32
                            sy.wait_ge(_sd_x, sd_x.n)

                    wb2 = [wbufa, wbufb]

                    def evac_qlat(ft, ps, sc, tok0):
                        return sc.activation(q_latT[:, ft * NQ:(ft + 1) * NQ],
                                             ps[:, 0:NQ], AF.Copy)

                    feat_proj(nc, sd_w, s_pe, s_ev, q_a_w.tensor, QR,
                              [(j * 128, 128) for j in range(12)],
                              wb2, xTq, NQ, 0, NQ, [psq0, psq1], evac_qlat)

                    with nc.Block() as blk:
                        ev0 = s_ev.n
                        pe0 = s_pe.n

                        @blk.scalar
                        def _(sc):
                            for rt in range(12):
                                sc.activation(sqq[:],
                                              q_latT[:, rt * NQ:(rt + 1) * NQ],
                                              AF.Square).then_inc(_s_ev, 1)
                                if rt < 11:
                                    sc.wait_ge(_s_pe, pe0 + rt + 1)
                        s_ev.n = ev0 + 12

                        @blk.tensor
                        def _(t):
                            for rt in range(12):
                                t.wait_ge(_s_ev, ev0 + rt + 1)
                                for ck in range(2):
                                    mm = t.matmul(
                                        psqb[0:1, ck * 512:(ck + 1) * 512],
                                        ones_f[:],
                                        sqq[:, ck * 512:(ck + 1) * 512],
                                        start=(rt == 0), stop=(rt == 11))
                                mm.then_inc(_s_pe, 1)
                        s_pe.n = pe0 + 12

                    with nc.Block() as blk:
                        @blk.scalar
                        def _(sc):
                            sc.activation(sqq[0:1, 0:NQ], psqb[0:1, 0:NQ], AF.Ln,
                                          bias=epsb[0:1, 0:1], scale=1.0 / QR)
                    with nc.Block() as blk:
                        @blk.scalar
                        def _(sc):
                            sc.activation(sqq[0:1, 0:NQ], sqq[0:1, 0:NQ], AF.Exp,
                                          scale=-0.5)
                    with nc.Block() as blk:
                        @blk.tensor
                        def _(t):
                            for ck in range(2):
                                t.matmul(psqb[:, ck * 512:(ck + 1) * 512],
                                         ones_row[:],
                                         sqq[0:1, ck * 512:(ck + 1) * 512],
                                         start=True, stop=True)
                    with nc.Block() as blk:
                        @blk.vector
                        def _(v):
                            for rt in range(12):
                                v.scalar_tensor_tensor(
                                    q_latT[:, rt * NQ:(rt + 1) * NQ],
                                    q_latT[:, rt * NQ:(rt + 1) * NQ],
                                    qanorm[:, rt:rt + 1],
                                    psqb[:, 0:NQ], ALU.mult, ALU.mult)

                    def evac_iq(ft, ps, sc, tok0):
                        return sc.activation(iqT[:, ft * NQ:(ft + 1) * NQ],
                                             ps[:, 0:NQ], AF.Copy)

                    feat_proj(nc, sd_w, s_pe, s_ev, idx_wq.tensor, IH * IDD,
                              [(j * 128, 128) for j in range(16)],
                              wb2, xTq, NQ, 0, NQ, [psq0, psq1], evac_iq)

                    def evac_iw(ft, ps, sc, tok0):
                        return sc.activation(iwst[:], ps[0:IH, 0:NQ], AF.Copy,
                                             scale=float(IH ** -0.5 * IDD ** -0.5))

                    feat_proj(nc, sd_w, s_pe, s_ev, idx_ww.tensor, IH,
                              [(0, 16)], wb2, xTq, NQ, 0, NQ, [psq0], evac_iw)

                    with nc.Block() as blk:
                        pe0 = s_pe.n
                        ev0 = s_ev.n

                        @blk.tensor
                        def _(t):
                            for qb in range(NQB):
                                if qb >= 1:
                                    t.wait_ge(_s_ev, ev0 + qb)
                                t.transpose(
                                    psqt[:, 0:16],
                                    iwst[0:IH, qb * 128:(qb + 1) * 128],
                                    ident[0:IH, 0:IH]).then_inc(_s_pe, 1)
                        s_pe.n = pe0 + NQB

                        @blk.vector
                        def _(v):
                            for qb in range(NQB):
                                v.wait_ge(_s_pe, pe0 + qb + 1)
                                v.tensor_copy(iw_t[:, qb * IH:(qb + 1) * IH],
                                              psqt[:, 0:16]
                                              ).then_inc(_s_ev, 1)
                        s_ev.n = ev0 + NQB

                    if t_qlatT is not None:
                        with nc.Block() as blk:
                            @blk.gpsimd
                            def _(sy):
                                sy.dma_start(t_qlatT[:], q_latT[:]
                                             ).then_inc(_sd_g, 16)
                                sy.dma_start(t_iqT[:], iqT[:]).then_inc(_sd_g, 16)
                                sy.dma_start(t_iwT[:], iw_t[:]).then_inc(_sd_g, 16)
                                sd_g.n += 48
                                sy.wait_ge(_sd_g, sd_g.n)

                with (
                    nc.sbuf_tensor("idx_acc", [128, ACC_W], F32,
                                   side="right") as idx_acc,
                    nc.sbuf_tensor("rb0", [128, 512], F32, side="right") as rb0,
                    nc.sbuf_tensor("rb1", [128, 512], F32, side="right") as rb1,
                    nc.sbuf_tensor("rb2", [128, 512], F32, side="right") as rb2,
                    nc.psum_tensor("psi0", [128, 512], F32) as psi0,
                    nc.psum_tensor("psi1", [128, 512], F32) as psi1,
                    nc.psum_tensor("psi2", [128, 512], F32) as psi2,
                ):
                    rbufs = [rb0, rb1, rb2]
                    psis = [psi0, psi1, psi2]
                    iters = []
                    for n in range(IH):
                        for qb in range(NQB):
                            for kc in range(KCS[qb]):
                                iters.append((n, qb, kc))
                    pe0, ev0, v0 = s_pe.n, s_ev.n, s_v2.n

                    with nc.Block() as blk:
                        @blk.tensor
                        def _(t):
                            for i, (n, qb, kc) in enumerate(iters):
                                if i >= 3:
                                    t.wait_ge(_s_ev, ev0 + i - 2)
                                t.matmul(psis[i % 3][:],
                                         iqT[:, n * NQ + qb * 128:
                                             n * NQ + (qb + 1) * 128],
                                         ikT[:, kc * 512:(kc + 1) * 512],
                                         start=True, stop=True
                                         ).then_inc(_s_pe, 1)
                        s_pe.n = pe0 + len(iters)

                        @blk.scalar
                        def _(sc):
                            for i, (n, qb, kc) in enumerate(iters):
                                sc.wait_ge(_s_pe, pe0 + i + 1)
                                if i >= 3:
                                    sc.wait_ge(_s_v2, v0 + i - 2)
                                sc.activation(rbufs[i % 3][:], psis[i % 3][:],
                                              AF.Relu).then_inc(_s_ev, 1)
                        s_ev.n = ev0 + len(iters)

                        @blk.vector
                        def _(v):
                            for i, (n, qb, kc) in enumerate(iters):
                                v.wait_ge(_s_ev, ev0 + i + 1)
                                dst = idx_acc[:, ACC_OFF[qb] + kc * 512:
                                              ACC_OFF[qb] + (kc + 1) * 512]
                                iws = iw_t[:, qb * IH + n:qb * IH + n + 1]
                                if n == 0:
                                    v.tensor_scalar(dst, rbufs[i % 3][:], iws,
                                                    None, ALU.mult
                                                    ).then_inc(_s_v2, 1)
                                else:
                                    v.scalar_tensor_tensor(dst, rbufs[i % 3][:],
                                                           iws, dst,
                                                           ALU.mult, ALU.add
                                                           ).then_inc(_s_v2, 1)
                        s_v2.n = v0 + len(iters)

                    if upto < 4:
                        if t_acc is not None:
                            with nc.Block() as blk:
                                @blk.gpsimd
                                def _(sy):
                                    sy.dma_start(t_acc[:], idx_acc[:]
                                                 ).then_inc(_sd_g, 16)
                                    sd_g.n += 16
                                    sy.wait_ge(_sd_g, sd_g.n)
                        _finish_stub(nc, out_loc, _sd_g, sd_g)
                        return nc

                    with (
                        nc.sbuf_tensor("abuf", [128, 8 * 512], F32,
                                       side="right") as abuf,
                        nc.sbuf_tensor("dump", [128, 2048], F32,
                                       side="right") as dump,
                        nc.sbuf_tensor("bs", [128, 64], F32,
                                       side="right") as bs,
                        nc.sbuf_tensor("kcol", [128, 512], F32,
                                       side="right") as kcol,
                    ):
                        with nc.Block() as blk:
                            @blk.gpsimd
                            def _(g):
                                g.iota(kcol[:], [[1, 512]], base=0,
                                       channel_multiplier=0,
                                       allow_small_or_imprecise_dtypes=True)
                        L = bs[:, 0:8]
                        Hh = bs[:, 8:16]
                        M2 = bs[:, 16:24]
                        CNT = bs[:, 24:32]
                        PRD = bs[:, 32:40]
                        P1M = bs[:, 40:48]
                        TMP = bs[:, 48:56]

                        with nc.Block() as blk:
                            @blk.gpsimd
                            def _(g):
                                for qb in range(NQB):
                                    g.tensor_scalar(
                                        abuf[:, qb * 512:(qb + 1) * 512],
                                        kcol[:], qposh_sb[:, qb:qb + 1],
                                        BIG_NEG, ALU.is_gt, ALU.mult)
                                g.memset(L[:], THR_LO)
                                g.memset(Hh[:], THR_HI)
                        with nc.Block() as blk:
                            @blk.vector
                            def _(v):
                                for qb in range(NQB):
                                    last = ACC_OFF[qb] + (KCS[qb] - 1) * 512
                                    v.tensor_add(
                                        idx_acc[:, last:last + 512],
                                        idx_acc[:, last:last + 512],
                                        abuf[:, qb * 512:(qb + 1) * 512])

                        for it in range(BISECT_ITERS):
                            with nc.Block() as blk:
                                @blk.vector
                                def _(v):
                                    v.tensor_add(M2[:], L[:], Hh[:])
                            with nc.Block() as blk:
                                @blk.vector
                                def _(v):
                                    v.tensor_scalar(M2[:], M2[:], 0.5, None,
                                                    ALU.mult)
                            with nc.Block() as blk:
                                @blk.vector
                                def _(v):
                                    for qb in range(NQB):
                                        w = KCS[qb] * 512
                                        v.tensor_scalar(
                                            dump[:, 0:w],
                                            idx_acc[:, ACC_OFF[qb]:
                                                    ACC_OFF[qb] + w],
                                            M2[:, qb:qb + 1], 0.0,
                                            ALU.is_gt, ALU.add,
                                            accum_out=CNT[:, qb:qb + 1])
                            with nc.Block() as blk:
                                @blk.vector
                                def _(v):
                                    v.tensor_scalar(PRD[:], CNT[:], float(TOPK),
                                                    None, ALU.is_ge)
                            with nc.Block() as blk:
                                @blk.vector
                                def _(v):
                                    v.tensor_scalar(P1M[:], PRD[:], -1.0, 1.0,
                                                    ALU.mult, ALU.add)
                                    v.tensor_sub(TMP[:], M2[:], L[:])
                            with nc.Block() as blk:
                                @blk.vector
                                def _(v):
                                    v.tensor_mul(TMP[:], TMP[:], PRD[:])
                            with nc.Block() as blk:
                                @blk.vector
                                def _(v):
                                    v.tensor_add(L[:], L[:], TMP[:])
                                    v.tensor_sub(TMP[:], M2[:], Hh[:])
                            with nc.Block() as blk:
                                @blk.vector
                                def _(v):
                                    v.tensor_mul(TMP[:], TMP[:], P1M[:])
                            with nc.Block() as blk:
                                @blk.vector
                                def _(v):
                                    v.tensor_add(Hh[:], Hh[:], TMP[:])

                        if t_thr is not None:
                            with nc.Block() as blk:
                                @blk.gpsimd
                                def _(sy):
                                    sy.dma_start(t_thr[:], L[:]
                                                 ).then_inc(_sd_g, 16)
                                    sy.dma_start(t_acc[:], idx_acc[:]
                                                 ).then_inc(_sd_g, 16)
                                    sd_g.n += 32
                                    sy.wait_ge(_sd_g, sd_g.n)

                        with (
                            nc.sbuf_tensor("mb0", [128, 2048], BF16,
                                           side="right") as mb0,
                            nc.sbuf_tensor("mb1", [128, 2048], BF16,
                                           side="right") as mb1,
                            nc.psum_tensor("pst0", [128, 128], BF16) as pst0,
                            nc.psum_tensor("pst1", [128, 128], BF16) as pst1,
                        ):
                            mbs = [mb0, mb1]
                            psts = [pst0, pst1]
                            v0 = s_v2.n
                            pe0 = s_pe.n
                            ev0 = s_ev.n
                            with nc.Block() as blk:
                                @blk.vector
                                def _(v):
                                    for qb in range(NQB):
                                        w = KCS[qb] * 512
                                        if qb >= 2:
                                            v.wait_ge(_s_pe, pe0 +
                                                      4 * sum(KCS[:qb - 1]))
                                        v.tensor_scalar(
                                            mbs[qb % 2][:, 0:w],
                                            idx_acc[:, ACC_OFF[qb]:
                                                    ACC_OFF[qb] + w],
                                            L[:, qb:qb + 1], None,
                                            ALU.is_gt
                                            ).then_inc(_s_v2, 1)
                                s_v2.n = v0 + NQB

                                @blk.tensor
                                def _(t):
                                    ti = 0
                                    for qb in range(NQB):
                                        t.wait_ge(_s_v2, v0 + qb + 1)
                                        for j in range(KCS[qb] * 4):
                                            if ti >= 2:
                                                t.wait_ge(_s_ev, ev0 + ti - 1)
                                            t.transpose(
                                                psts[ti % 2][:],
                                                mbs[qb % 2][:, j * 128:
                                                            (j + 1) * 128],
                                                ident[:]).then_inc(_s_pe, 1)
                                            ti += 1
                                    s_pe.n = pe0 + ti

                                @blk.scalar
                                def _(sc):
                                    ti = 0
                                    for qb in range(NQB):
                                        for j in range(KCS[qb] * 4):
                                            sc.wait_ge(_s_pe, pe0 + ti + 1)
                                            sc.activation(
                                                maskT[:, j * NQ + qb * 128:
                                                      j * NQ + (qb + 1) * 128],
                                                psts[ti % 2][:], AF.Copy
                                                ).then_inc(_s_ev, 1)
                                            ti += 1
                                    s_ev.n = ev0 + ti

                            if t_maskT is not None:
                                with nc.Block() as blk:
                                    @blk.gpsimd
                                    def _(sy):
                                        sy.dma_start(t_maskT[:], maskT[:]
                                                     ).then_inc(_sd_g, 16)
                                        sd_g.n += 16
                                        sy.wait_ge(_sd_g, sd_g.n)

                if upto < 6:
                    _finish_stub(nc, out_loc, _sd_g, sd_g)
                    return nc

                stkI.close()

                with ExitStack() as stk6:
                    e6x = stk6.enter_context
                    attn_oT = e6x(nc.sbuf_tensor("attn_oT", [128, H * NQ], BF16,
                                                 side="left"))
                    stk6r = e6x(ExitStack())
                    e6 = stk6r.enter_context
                    cosq = e6(nc.sbuf_tensor("cosq", [32, NQ], F32, side="right"))
                    sinq = e6(nc.sbuf_tensor("sinq", [32, NQ], F32, side="right"))
                    rsc0 = e6(nc.sbuf_tensor("rsc0", [32, NQ], F32, side="right"))
                    rsc1 = e6(nc.sbuf_tensor("rsc1", [32, NQ], F32, side="right"))
                    qnT = e6(nc.sbuf_tensor("qnT", [128, NQ], BF16, side="right"))
                    qpeT = e6(nc.sbuf_tensor("qpeT", [64, NQ], BF16, side="right"))
                    knT = e6(nc.sbuf_tensor("knT", [128, S], BF16, side="right"))
                    v_h = e6(nc.sbuf_tensor("v_h", [128, S], BF16, side="right"))
                    pb0 = e6(nc.sbuf_tensor("pb0", [128, 512], BF16, side="right"))
                    pb1 = e6(nc.sbuf_tensor("pb1", [128, 512], BF16, side="right"))
                    pb2 = e6(nc.sbuf_tensor("pb2", [128, 512], BF16, side="right"))
                    qp0 = e6(nc.sbuf_tensor("qp0", [32, NQ], BF16, side="right"))
                    qp1 = e6(nc.sbuf_tensor("qp1", [32, NQ], BF16, side="right"))
                    dnm = e6(nc.sbuf_tensor("dnm", [128, NQ], F32, side="right"))
                    bcb = e6(nc.sbuf_tensor("bcb", [128, NQ], F32, side="right"))
                    wq0 = e6(nc.sbuf_tensor("wq0", [128, 12 * QKD], BF16,
                                            side="right"))
                    wq1 = e6(nc.sbuf_tensor("wq1", [128, 12 * QKD], BF16,
                                            side="right"))
                    wkv0 = e6(nc.sbuf_tensor("wkv0", [128, 4 * 256], BF16,
                                             side="right"))
                    wkv1 = e6(nc.sbuf_tensor("wkv1", [128, 4 * 256], BF16,
                                             side="right"))
                    psA = e6(nc.psum_tensor("psA", [128, 2048], F32))
                    psB = e6(nc.psum_tensor("psB", [128, 2048], F32))
                    
                    pbufs = [pb0, pb1, pb2]
                    wqs = [wq0, wq1]
                    wkvs = [wkv0, wkv1]
                    SCALE_QK = float(QKD ** -0.5)

                    if p6_setup >= 0:
                        with nc.Block() as blk:
                            @blk.gpsimd
                            def _(g):
                                if p6_setup >= 0.5:
                                    g.memset(attn_oT[:], 0.0)
                                g.dma_start(rsc0[:],
                                            qpos_row.broadcast_to([32, NQ])
                                            ).then_inc(_sd_g, 16)
                                sd_g.n += 16
                                g.wait_ge(_sd_g, sd_g.n)
                    if p6_setup >= 1:
                        build_rope_tables(nc, sd_x, rsc0[:], NQ, cosq[:],
                                          sinq[:], invf, rsc0[:], rsc1[:])

                    if p6_setup >= 2:
                        with nc.Block() as blk:
                            @blk.gpsimd
                            def _(g):
                                srcw = bass.AP(q_b_w.tensor, 0,
                                               [[H * QKD, 128],
                                                [H * QKD * 128, 12], [1, QKD]])
                                g.dma_start(wq0[:], srcw).then_inc(_sd_w, 16)
                                srcw = bass.AP(kv_b_w.tensor, 0,
                                               [[H * (NOPE + VD), 128],
                                                [H * (NOPE + VD) * 128, 4],
                                                [1, 256]])
                                g.dma_start(wkv0[:], srcw).then_inc(_sd_w, 16)
                                sd_w.n += 32
                                g.wait_ge(_sd_w, sd_w.n)

                    W0 = sd_w.n
                    for h in range(p6_heads):
                        wq = wqs[h % 2]
                        wkv = wkvs[h % 2]

                        with nc.Block() as blk:
                            pe0, ev0, v0 = s_pe.n, s_ev.n, s_v2.n

                            @blk.gpsimd
                            def _(g):
                                if h + 1 < H:
                                    srcq = bass.AP(
                                        q_b_w.tensor, (h + 1) * QKD,
                                        [[H * QKD, 128],
                                         [H * QKD * 128, 12], [1, QKD]])
                                    g.dma_start(wqs[(h + 1) % 2][:], srcq
                                                ).then_inc(_sd_w, 16)
                                    srck = bass.AP(
                                        kv_b_w.tensor, (h + 1) * 256,
                                        [[H * (NOPE + VD), 128],
                                         [H * (NOPE + VD) * 128, 4], [1, 256]])
                                    g.dma_start(wkvs[(h + 1) % 2][:], srck
                                                ).then_inc(_sd_w, 16)
                                    sd_w.n += 32

                            @blk.tensor
                            def _(t):
                                t.wait_ge(_sd_w, W0 + 32 * h)
                                for rt in range(12):
                                    for qc in range(2):
                                        mm = t.matmul(
                                            psA[:, qc * 512:(qc + 1) * 512],
                                            wq[:, rt * QKD:rt * QKD + 128],
                                            q_latT[:, rt * NQ + qc * 512:
                                                   rt * NQ + (qc + 1) * 512],
                                            start=(rt == 0), stop=(rt == 11))
                                mm.then_inc(_s_pe, 1)
                                for rt in range(12):
                                    for qc in range(2):
                                        mm = t.matmul(
                                            psA[0:64, 1024 + qc * 512:
                                                1024 + (qc + 1) * 512],
                                            wq[:, rt * QKD + 128:
                                               rt * QKD + QKD],
                                            q_latT[:, rt * NQ + qc * 512:
                                                   rt * NQ + (qc + 1) * 512],
                                            start=(rt == 0), stop=(rt == 11))
                                mm.then_inc(_s_pe, 1)
                                if p6_sub < 2:
                                    t.nop().then_inc(_s_pe, 1)
                                    for tb in range(16):
                                        t.nop().then_inc(_s_pe, 1)
                                    return
                                for rt in range(4):
                                    for ck in range(4):
                                        mm = t.matmul(
                                            psB[:, ck * 512:(ck + 1) * 512],
                                            wkv[:, rt * 256:rt * 256 + 128],
                                            kvnT[:, rt * S + ck * 512:
                                                 rt * S + (ck + 1) * 512],
                                            start=(rt == 0), stop=(rt == 3))
                                mm.then_inc(_s_pe, 1)
                            s_pe.n = pe0 + 3

                            @blk.scalar
                            def _(sc):
                                sc.wait_ge(_s_pe, pe0 + 1)
                                sc.activation(qnT[:], psA[:, 0:NQ], AF.Copy
                                              ).then_inc(_s_ev, 1)
                                sc.wait_ge(_s_pe, pe0 + 3)
                                if p6_sub >= 2:
                                    sc.activation(knT[:], psB[:, 0:S], AF.Copy
                                                  ).then_inc(_s_ev, 1)
                                else:
                                    sc.nop().then_inc(_s_ev, 1)
                                sc.wait_ge(_s_pe, pe0 + 2)
                                sc.activation(qp0[:], psA[0:32, 1024:2048],
                                              AF.Copy)
                                sc.activation(qp1[:], psA[32:64, 1024:2048],
                                              AF.Copy).then_inc(_s_ev, 1)
                                sc.wait_ge(_s_v2, v0 + 3)
                                sc.activation(qpeT[32:64, :], rsc0[:], AF.Copy
                                              ).then_inc(_s_ev, 1)
                            s_ev.n = ev0 + 4

                            @blk.vector
                            def _(v):
                                v.wait_ge(_s_ev, ev0 + 3)
                                v.tensor_mul(rsc0[:], qp1[:], sinq[:])
                                v.tensor_mul(rsc1[:], qp0[:], cosq[:]
                                             ).then_inc(_s_v2, 1)
                                v.wait_ge(_s_v2, v0 + 1)
                                v.tensor_sub(qpeT[0:32, :], rsc1[:], rsc0[:])
                                v.tensor_mul(rsc0[:], qp0[:], sinq[:])
                                v.tensor_mul(rsc1[:], qp1[:], cosq[:]
                                             ).then_inc(_s_v2, 1)
                                v.wait_ge(_s_v2, v0 + 2)
                                v.tensor_add(rsc0[:], rsc0[:], rsc1[:]
                                             ).then_inc(_s_v2, 1)
                            s_v2.n = v0 + 3

                        with nc.Block() as blk:
                            pe0, ev0 = s_pe.n, s_ev.n

                            @blk.tensor
                            def _(t):
                                if p6_sub >= 3:
                                    for tb in range(16):
                                        if tb >= 4:
                                            t.wait_ge(_s_ev, ev0 + tb - 3)
                                        for rt in range(4):
                                            mm = t.matmul(
                                                psB[:, (tb % 4) * 512:
                                                    (tb % 4) * 512 + 128],
                                                kvnT[:, rt * S + tb * 128:
                                                     rt * S + (tb + 1) * 128],
                                                wkv[:, rt * 256 + 128:
                                                    rt * 256 + 256],
                                                start=(rt == 0), stop=(rt == 3))
                                        mm.then_inc(_s_pe, 1)
                            s_pe.n = pe0 + (16 if p6_sub >= 3 else 0)

                            @blk.scalar
                            def _(sc):
                                if p6_sub >= 3:
                                    for tb in range(16):
                                        sc.wait_ge(_s_pe, pe0 + tb + 1)
                                        sc.activation(
                                            v_h[:, tb * 128:(tb + 1) * 128],
                                            psB[:, (tb % 4) * 512:
                                                (tb % 4) * 512 + 128], AF.Copy
                                            ).then_inc(_s_ev, 1)
                            s_ev.n = ev0 + (16 if p6_sub >= 3 else 0)

                        if p6_stages < 2:
                            continue
                        with nc.Block() as blk:
                            pe0, ev0, v0, g0 = s_pe.n, s_ev.n, s_v2.n, s_gp.n
                            jobs = []
                            for qc in range(2):
                                for kb in range(KB_PER_QC[qc]):
                                    jobs.append((qc, kb))
                            nj = len(jobs)
                            pe_pos_sT = {}
                            pe_pos_av = {}
                            cnt = 0
                            seqc = []
                            seqc.append(("sT", 0))
                            for j in range(1, nj):
                                seqc.append(("sT", j))
                                seqc.append(("av", j - 1))
                            seqc.append(("av", nj - 1))
                            for kind, j in seqc:
                                cnt += 1
                                if kind == "sT":
                                    pe_pos_sT[j] = cnt
                                else:
                                    pe_pos_av[j] = cnt

                            @blk.tensor
                            def _(t):
                                def do_sT(j):
                                    qc, kb = jobs[j]
                                    slot = psA[:, (j % 2) * 512:
                                               (j % 2) * 512 + 512]
                                    if j >= 2:
                                        t.wait_ge(_s_ev, ev0 + j - 1)
                                    t.matmul(slot,
                                             knT[:, kb * 128:(kb + 1) * 128],
                                             qnT[:, qc * 512:(qc + 1) * 512],
                                             start=True, stop=False)
                                    t.matmul(slot,
                                             k_peT[:, kb * 128:(kb + 1) * 128],
                                             qpeT[:, qc * 512:(qc + 1) * 512],
                                             start=False, stop=True
                                             ).then_inc(_s_pe, 1)

                                def do_av(j):
                                    qc, kb = jobs[j]
                                    t.wait_ge(_s_v2, v0 + j + 1)
                                    t.matmul(psB[:, qc * 512:(qc + 1) * 512],
                                             v_h[:, kb * 128:(kb + 1) * 128],
                                             pbufs[j % 3][:],
                                             start=(kb == 0),
                                             stop=(kb == KB_PER_QC[qc] - 1)
                                             ).then_inc(_s_pe, 1)

                                for kind, j in seqc:
                                    if kind == "sT":
                                        do_sT(j)
                                    else:
                                        do_av(j)
                            s_pe.n = pe0 + cnt

                            @blk.scalar
                            def _(sc):
                                for j in range(nj):
                                    sc.wait_ge(_s_pe, pe0 + pe_pos_sT[j])
                                    if j >= 3:
                                        sc.wait_ge(_s_pe, pe0 + pe_pos_av[j - 3])
                                        sc.wait_ge(_s_gp, g0 + j - 2)
                                    sc.activation(pbufs[j % 3][:],
                                                  psA[:, (j % 2) * 512:
                                                      (j % 2) * 512 + 512],
                                                  AF.Exp, scale=SCALE_QK
                                                  ).then_inc(_s_ev, 1)
                            s_ev.n = ev0 + nj

                            @blk.vector
                            def _(v):
                                for j in range(nj):
                                    qc, kb = jobs[j]
                                    v.wait_ge(_s_ev, ev0 + j + 1)
                                    v.tensor_mul(pbufs[j % 3][:], pbufs[j % 3][:],
                                                 maskT[:, kb * NQ + qc * 512:
                                                       kb * NQ + (qc + 1) * 512]
                                                 ).then_inc(_s_v2, 1)
                            s_v2.n = v0 + nj

                            @blk.gpsimd
                            def _(g):
                                for j in range(nj):
                                    qc, kb = jobs[j]
                                    g.wait_ge(_s_v2, v0 + j + 1)
                                    dst = dnm[:, qc * 512:(qc + 1) * 512]
                                    if kb == 0:
                                        g.tensor_copy(dst, pbufs[j % 3][:]
                                                      ).then_inc(_s_gp, 1)
                                    else:
                                        g.tensor_add(dst, dst, pbufs[j % 3][:]
                                                     ).then_inc(_s_gp, 1)
                            s_gp.n = g0 + nj

                        if p6_stages < 3:
                            continue
                        with nc.Block() as blk:
                            pe0, ev0, v0 = s_pe.n, s_ev.n, s_v2.n

                            @blk.tensor
                            def _(t):
                                for qc in range(2):
                                    t.matmul(psA[0:1, 1024 + qc * 512:
                                                  1024 + (qc + 1) * 512],
                                             ones_f[:],
                                             dnm[:, qc * 512:(qc + 1) * 512],
                                             start=True, stop=True
                                             ).then_inc(_s_pe, 1)
                                t.wait_ge(_s_v2, v0 + 1)
                                for qc in range(2):
                                    t.matmul(psA[:, 1024 + qc * 512:
                                                 1024 + (qc + 1) * 512],
                                             ones_row[:],
                                             bcb[0:1, qc * 512:(qc + 1) * 512],
                                             start=True, stop=True
                                             ).then_inc(_s_pe, 1)
                            s_pe.n = pe0 + 4

                            @blk.vector
                            def _(v):
                                v.wait_ge(_s_pe, pe0 + 2)
                                v.reciprocal(bcb[0:1, 0:NQ],
                                             psA[0:1, 1024:2048]
                                             ).then_inc(_s_v2, 1)
                                v.wait_ge(_s_pe, pe0 + 4)
                                v.tensor_copy(bcb[:], psA[:, 1024:2048]
                                              ).then_inc(_s_v2, 1)
                                v.wait_ge(_s_v2, v0 + 2)
                                v.tensor_mul(attn_oT[:, h * NQ:(h + 1) * NQ],
                                             psB[:, 0:NQ], bcb[:]
                                             ).then_inc(_s_v2, 1)
                            s_v2.n = v0 + 3

                    if t_attnoT is not None:
                        with nc.Block() as blk:
                            @blk.gpsimd
                            def _(sy):
                                sy.dma_start(t_attnoT[:], attn_oT[:]
                                             ).then_inc(_sd_g, 16)
                                sd_g.n += 16
                                sy.wait_ge(_sd_g, sd_g.n)

                    if upto < 7:
                        _finish_stub(nc, out_loc, _sd_g, sd_g)
                        return nc

                    stk6r.close()

                    with (
                        nc.sbuf_tensor("ot0", [128, 16 * 512], BF16,
                                       side="right") as ot0,
                        nc.sbuf_tensor("ot1", [128, 16 * 512], BF16,
                                       side="right") as ot1,
                        nc.sbuf_tensor("ost0", [128, 512], F32,
                                       side="right") as ost0,
                        nc.sbuf_tensor("ost1", [128, 512], F32,
                                       side="right") as ost1,
                        nc.psum_tensor("psO", [128, 4096], F32) as psO,
                    ):
                        ots = [ot0, ot1]
                        osts = [ost0, ost1]
                        pe0, ev0, w0, o0 = s_pe.n, s_ev.n, sd_w.n, sd_o.n
                        with nc.Block() as blk:
                            @blk.gpsimd
                            def _(g):
                                for dc in range(8):
                                    for hg in range(2):
                                        i = dc * 2 + hg
                                        if i >= 2:
                                            g.wait_ge(_s_pe, pe0 + i - 1)
                                        srco = bass.AP(
                                            o_w.tensor,
                                            hg * 16 * 128 * HID + dc * 512,
                                            [[HID, 128], [HID * 128, 16],
                                             [1, 512]])
                                        g.dma_start(ots[i % 2][:], srco
                                                    ).then_inc(_sd_w, 16)
                            sd_w.n = w0 + 16 * 16

                            @blk.tensor
                            def _(t):
                                for dc in range(8):
                                    if dc >= 1:
                                        t.wait_ge(_s_ev, ev0 + dc * 8)
                                    for hg in range(2):
                                        i = dc * 2 + hg
                                        t.wait_ge(_sd_w, w0 + 16 * (i + 1))
                                        for hh in range(16):
                                            h = hg * 16 + hh
                                            for tb in range(8):
                                                mm = t.matmul(
                                                    psO[:, tb * 512:
                                                        (tb + 1) * 512],
                                                    attn_oT[:, h * NQ + tb * 128:
                                                            h * NQ +
                                                            (tb + 1) * 128],
                                                    ots[i % 2][:, hh * 512:
                                                               (hh + 1) * 512],
                                                    start=(h == 0),
                                                    stop=(h == H - 1))
                                        mm.then_inc(_s_pe, 1)
                            s_pe.n = pe0 + 16

                            @blk.scalar
                            def _(sc):
                                for dc in range(8):
                                    sc.wait_ge(_s_pe, pe0 + (dc + 1) * 2)
                                    for tb in range(8):
                                        if dc * 8 + tb >= 2:
                                            sc.wait_ge(_sd_o,
                                                       o0 + 16 * (dc * 8 + tb - 1))
                                        sc.activation(osts[tb % 2][:],
                                                      psO[:, tb * 512:
                                                          (tb + 1) * 512],
                                                      AF.Copy
                                                      ).then_inc(_s_ev, 1)
                            s_ev.n = ev0 + 64

                            @blk.sync
                            def _(sy):
                                for dc in range(8):
                                    for tb in range(8):
                                        sy.wait_ge(_s_ev, ev0 + dc * 8 + tb + 1)
                                        sy.dma_start(
                                            attn_pre[tb * 128:(tb + 1) * 128,
                                                     dc * 512:(dc + 1) * 512],
                                            osts[tb % 2][:]).then_inc(_sd_o, 16)
                                sy.wait_ge(_sd_o, o0 + 16 * 64)
                            sd_o.n = o0 + 16 * 64

                with (
                    nc.sbuf_tensor("ra0", [128, HID], F32, side="right") as ra0,
                    nc.sbuf_tensor("ra1", [128, HID], F32, side="right") as ra1,
                    nc.sbuf_tensor("rb0_", [128, HID], F32, side="right") as rb0_,
                    nc.sbuf_tensor("rb1_", [128, HID], F32, side="right") as rb1_,
                    nc.sbuf_tensor("dmp", [128, HID], F32, side="right") as dmp,
                    nc.sbuf_tensor("lnt", [128, 8], F32, side="right") as lnt,
                ):
                    ras = [ra0, ra1]
                    rbs = [rb0_, rb1_]
                    x0_, v0, ev0, g0 = sd_x.n, s_v2.n, s_ev.n, s_gp.n
                    o0_ = sd_g.n
                    with nc.Block() as blk:
                        @blk.sync
                        def _(sy):
                            for tb in range(NQB):
                                if tb >= 2:
                                    sy.wait_ge(_sd_g, o0_ + 32 * (tb - 1))
                                sy.dma_start(ras[tb % 2][:],
                                             attn_pre[tb * 128:(tb + 1) * 128, :]
                                             ).then_inc(_sd_x, 16)
                                sy.dma_start(rbs[tb % 2][:],
                                             x_q[tb * 128:(tb + 1) * 128, :]
                                             ).then_inc(_sd_x, 16)
                        sd_x.n = x0_ + 16 * 2 * NQB

                        @blk.vector
                        def _(v):
                            for tb in range(NQB):
                                r = tb % 2
                                v.wait_ge(_sd_x, x0_ + 16 * 2 * (tb + 1))
                                v.tensor_add(ras[r][:], ras[r][:], rbs[r][:]
                                             ).then_inc(_s_v2, 1)
                                v.wait_ge(_s_v2, v0 + 4 * tb + 1)
                                v.reduce_sum(lnt[:, r:r + 1], ras[r][:], AX.X
                                             ).then_inc(_s_v2, 1)
                                v.wait_ge(_s_v2, v0 + 4 * tb + 2)
                                v.tensor_scalar(lnt[:, 2 + r:3 + r],
                                                lnt[:, r:r + 1],
                                                -1.0 / HID, None, ALU.mult
                                                ).then_inc(_s_v2, 1)
                                v.wait_ge(_s_ev, ev0 + 2 * tb + 2)
                                v.tensor_scalar(ras[r][:], ras[r][:],
                                                lnt[:, 2 + r:3 + r],
                                                lnt[:, 6 + r:7 + r],
                                                ALU.add, ALU.mult
                                                ).then_inc(_s_v2, 1)
                        s_v2.n = v0 + 4 * NQB

                        @blk.scalar
                        def _(sc):
                            for tb in range(NQB):
                                r = tb % 2
                                sc.wait_ge(_s_v2, v0 + 4 * tb + 3)
                                sc.activation(dmp[:], ras[r][:], AF.Square,
                                              bias=lnt[:, 2 + r:3 + r],
                                              accum_out=lnt[:, 4 + r:5 + r]
                                              ).then_inc(_s_ev, 1)
                                sc.wait_ge(_s_ev, ev0 + 2 * tb + 1)
                                sc.activation(lnt[:, 6 + r:7 + r],
                                              lnt[:, 4 + r:5 + r], AF.Ln,
                                              bias=epsb[:, 0:1],
                                              scale=1.0 / HID)
                                sc.activation(lnt[:, 6 + r:7 + r],
                                              lnt[:, 6 + r:7 + r], AF.Exp,
                                              scale=-0.5).then_inc(_s_ev, 1)
                        s_ev.n = ev0 + 2 * NQB

                        @blk.gpsimd
                        def _(g):
                            for tb in range(NQB):
                                r = tb % 2
                                g.wait_ge(_s_v2, v0 + 4 * tb + 4)
                                g.dma_start(x1f[tb * 128:(tb + 1) * 128, :],
                                            ras[r][:]).then_inc(_sd_g, 16)
                                g.dma_start(x1bf[tb * 128:(tb + 1) * 128, :],
                                            ras[r][:]).then_inc(_sd_g, 16)
                            g.wait_ge(_sd_g, o0_ + 32 * NQB)
                        sd_g.n = o0_ + 32 * NQB

                if t_x1 is not None:
                    with nc.Block() as blk:
                        @blk.sync
                        def _(sy):
                            for tb in range(NQB):
                                sy.dma_start(t_x1[tb * 128:(tb + 1) * 128, :],
                                             x1f[tb * 128:(tb + 1) * 128, :]
                                             ).then_inc(_sd_o, 16)
                            sd_o.n += 16 * NQB
                            sy.wait_ge(_sd_o, sd_o.n)

        if upto < 9:
            _finish_stub(nc, out_loc, _sd_g, sd_g)
            return nc

        NIB = II // 128
        with (
            nc.sbuf_tensor("x1T", [128, 32 * NQ], BF16, side="left") as x1T,
            nc.sbuf_tensor("wg0", [128, 32 * 128], BF16, side="right") as wg0,
            nc.sbuf_tensor("wg1", [128, 32 * 128], BF16, side="right") as wg1,
            nc.sbuf_tensor("wu0", [128, 32 * 128], BF16, side="right") as wu0,
            nc.sbuf_tensor("wu1", [128, 32 * 128], BF16, side="right") as wu1,
            nc.sbuf_tensor("gs0", [128, NQ], F32, side="right") as gs0,
            nc.sbuf_tensor("gs1", [128, NQ], F32, side="right") as gs1,
            nc.sbuf_tensor("ht0", [128, NQ], BF16, side="right") as ht0,
            nc.sbuf_tensor("ht1", [128, NQ], BF16, side="right") as ht1,
            nc.psum_tensor("psG", [128, 2048], F32) as psG,
            nc.psum_tensor("psU", [128, 2048], F32) as psU,
        ):
            wgs = [wg0, wg1]
            wus = [wu0, wu1]
            gss = [gs0, gs1]
            hts = [ht0, ht1]
            with nc.Block() as blk:
                @blk.sync
                def _(sy):
                    for db in range(32):
                        sy.dma_start(x1T[:, db * NQ:(db + 1) * NQ],
                                     x1bf[:, db * 128:(db + 1) * 128],
                                     transpose=True).then_inc(_sd_x, 16)
                    sd_x.n += 16 * 32
                    sy.wait_ge(_sd_x, sd_x.n)

            pe0, ev0, v0, w0, o0 = s_pe.n, s_ev.n, s_v2.n, sd_w.n, sd_o.n
            with nc.Block() as blk:
                @blk.gpsimd
                def _(g):
                    for ib in range(NIB):
                        if ib >= 2:
                            g.wait_ge(_s_pe, pe0 + 2 * (ib - 1))
                        srcg = bass.AP(gate_wT.tensor, ib * 128,
                                       [[II, 128], [II * 128, 32], [1, 128]])
                        g.dma_start(wgs[ib % 2][:], srcg).then_inc(_sd_w, 16)
                        srcu = bass.AP(up_wT.tensor, ib * 128,
                                       [[II, 128], [II * 128, 32], [1, 128]])
                        g.dma_start(wus[ib % 2][:], srcu).then_inc(_sd_w, 16)
                sd_w.n = w0 + 32 * NIB

                @blk.tensor
                def _(t):
                    for ib in range(NIB):
                        t.wait_ge(_sd_w, w0 + 32 * (ib + 1))
                        if ib >= 2:
                            t.wait_ge(_s_ev, ev0 + ib - 1)
                            t.wait_ge(_s_v2, v0 + ib - 1)
                        po = (ib % 2) * 1024
                        for db in range(32):
                            for qc in range(2):
                                mm = t.matmul(
                                    psG[:, po + qc * 512:po + (qc + 1) * 512],
                                    wgs[ib % 2][:, db * 128:(db + 1) * 128],
                                    x1T[:, db * NQ + qc * 512:
                                        db * NQ + (qc + 1) * 512],
                                    start=(db == 0), stop=(db == 31))
                        mm.then_inc(_s_pe, 1)
                        for db in range(32):
                            for qc in range(2):
                                mm = t.matmul(
                                    psU[:, po + qc * 512:po + (qc + 1) * 512],
                                    wus[ib % 2][:, db * 128:(db + 1) * 128],
                                    x1T[:, db * NQ + qc * 512:
                                        db * NQ + (qc + 1) * 512],
                                    start=(db == 0), stop=(db == 31))
                        mm.then_inc(_s_pe, 1)
                s_pe.n = pe0 + 2 * NIB

                @blk.scalar
                def _(sc):
                    for ib in range(NIB):
                        po = (ib % 2) * 1024
                        sc.wait_ge(_s_pe, pe0 + 2 * ib + 1)
                        if ib >= 2:
                            sc.wait_ge(_s_v2, v0 + ib - 1)
                        sc.activation(gss[ib % 2][:], psG[:, po:po + NQ],
                                      AF.Silu, scale=scl[:, 0:1]
                                      ).then_inc(_s_ev, 1)
                s_ev.n = ev0 + NIB

                @blk.vector
                def _(v):
                    for ib in range(NIB):
                        po = (ib % 2) * 1024
                        v.wait_ge(_s_ev, ev0 + ib + 1)
                        v.wait_ge(_s_pe, pe0 + 2 * ib + 2)
                        if ib >= 2:
                            v.wait_ge(_sd_o, o0 + 16 * (ib - 1))
                        v.scalar_tensor_tensor(hts[ib % 2][:],
                                               psU[:, po:po + NQ],
                                               scl[:, 1:2], gss[ib % 2][:],
                                               ALU.mult, ALU.mult
                                               ).then_inc(_s_v2, 1)
                s_v2.n = v0 + NIB

                @blk.sync
                def _(sy):
                    for ib in range(NIB):
                        sy.wait_ge(_s_v2, v0 + ib + 1)
                        sy.dma_start(hTd[ib * 128:(ib + 1) * 128, :],
                                     hts[ib % 2][:]).then_inc(_sd_o, 16)
                    sy.wait_ge(_sd_o, o0 + 16 * NIB)
                sd_o.n = o0 + 16 * NIB

        if upto < 10:
            _finish_stub(nc, out_loc, _sd_g, sd_g)
            return nc

        with (
            nc.sbuf_tensor("hT", [128, NIB * NQ], BF16, side="left") as hT,
            nc.sbuf_tensor("wd0", [128, 8 * 512], BF16, side="right") as wd0,
            nc.sbuf_tensor("wd1", [128, 8 * 512], BF16, side="right") as wd1,
            nc.sbuf_tensor("ds0", [128, 512], F32, side="right") as ds0,
            nc.sbuf_tensor("ds1", [128, 512], F32, side="right") as ds1,
            nc.psum_tensor("psD", [128, 4096], F32) as psD,
        ):
            wds = [wd0, wd1]
            dss = [ds0, ds1]
            with nc.Block() as blk:
                @blk.sync
                def _(sy):
                    for ib in range(NIB):
                        sy.dma_start(hT[:, ib * NQ:(ib + 1) * NQ],
                                     hTd[ib * 128:(ib + 1) * 128, :]
                                     ).then_inc(_sd_x, 16)
                    sd_x.n += 16 * NIB
                    sy.wait_ge(_sd_x, sd_x.n)

            NIG = 11
            IGS = [8] * 10 + [6]
            pe0, ev0, w0, o0 = s_pe.n, s_ev.n, sd_w.n, sd_o.n
            with nc.Block() as blk:
                @blk.gpsimd
                def _(g):
                    i = 0
                    for dc in range(8):
                        for ig in range(NIG):
                            if i >= 2:
                                g.wait_ge(_s_pe, pe0 + i - 1)
                            nib = IGS[ig]
                            srcd = bass.AP(
                                down_wT.tensor,
                                (ig * 8 * 128) * HID + dc * 512,
                                [[HID, 128], [HID * 128, nib], [1, 512]])
                            g.dma_start(wds[i % 2][:, 0:nib * 512], srcd
                                        ).then_inc(_sd_w, 16)
                            i += 1
                sd_w.n = w0 + 16 * 8 * NIG

                @blk.tensor
                def _(t):
                    i = 0
                    for dc in range(8):
                        if dc >= 1:
                            t.wait_ge(_s_ev, ev0 + dc * 8)
                        for ig in range(NIG):
                            t.wait_ge(_sd_w, w0 + 16 * (i + 1))
                            nib = IGS[ig]
                            for k in range(nib):
                                ib = ig * 8 + k
                                for tb in range(8):
                                    mm = t.matmul(
                                        psD[:, tb * 512:(tb + 1) * 512],
                                        hT[:, ib * NQ + tb * 128:
                                           ib * NQ + (tb + 1) * 128],
                                        wds[i % 2][:, k * 512:(k + 1) * 512],
                                        start=(ib == 0), stop=(ib == NIB - 1))
                            mm.then_inc(_s_pe, 1)
                            i += 1
                s_pe.n = pe0 + 8 * NIG

                @blk.scalar
                def _(sc):
                    for dc in range(8):
                        sc.wait_ge(_s_pe, pe0 + (dc + 1) * NIG)
                        for tb in range(8):
                            if dc * 8 + tb >= 2:
                                sc.wait_ge(_sd_o, o0 + 16 * (dc * 8 + tb - 1))
                            sc.activation(dss[tb % 2][:],
                                          psD[:, tb * 512:(tb + 1) * 512],
                                          AF.Copy, scale=scl[:, 2:3]
                                          ).then_inc(_s_ev, 1)
                s_ev.n = ev0 + 64

                @blk.sync
                def _(sy):
                    for dc in range(8):
                        for tb in range(8):
                            sy.wait_ge(_s_ev, ev0 + dc * 8 + tb + 1)
                            sy.dma_start(
                                ffn_pre[tb * 128:(tb + 1) * 128,
                                        dc * 512:(dc + 1) * 512],
                                dss[tb % 2][:]).then_inc(_sd_o, 16)
                    sy.wait_ge(_sd_o, o0 + 16 * 64)
                sd_o.n = o0 + 16 * 64

        if upto < 11:
            _finish_stub(nc, out_loc, _sd_g, sd_g)
            return nc

        with (
            nc.sbuf_tensor("fa0", [128, HID], F32, side="right") as fa0,
            nc.sbuf_tensor("fa1", [128, HID], F32, side="right") as fa1,
            nc.sbuf_tensor("fb0", [128, HID], F32, side="right") as fb0,
            nc.sbuf_tensor("fb1", [128, HID], F32, side="right") as fb1,
            nc.sbuf_tensor("fdmp", [128, HID], F32, side="right") as fdmp,
            nc.sbuf_tensor("flnt", [128, 8], F32, side="right") as flnt,
        ):
            fas = [fa0, fa1]
            fbs = [fb0, fb1]
            x0_, v0, ev0, o0 = sd_x.n, s_v2.n, s_ev.n, sd_g.n
            with nc.Block() as blk:
                @blk.sync
                def _(sy):
                    for tb in range(NQB):
                        if tb >= 2:
                            sy.wait_ge(_sd_g, o0 + 16 * (tb - 1))
                        sy.dma_start(fas[tb % 2][:],
                                     ffn_pre[tb * 128:(tb + 1) * 128, :]
                                     ).then_inc(_sd_x, 16)
                        sy.dma_start(fbs[tb % 2][:],
                                     x1f[tb * 128:(tb + 1) * 128, :]
                                     ).then_inc(_sd_x, 16)
                sd_x.n = x0_ + 16 * 2 * NQB

                @blk.vector
                def _(v):
                    for tb in range(NQB):
                        r = tb % 2
                        v.wait_ge(_sd_x, x0_ + 16 * 2 * (tb + 1))
                        v.tensor_add(fas[r][:], fas[r][:], fbs[r][:]
                                     ).then_inc(_s_v2, 1)
                        v.wait_ge(_s_v2, v0 + 4 * tb + 1)
                        v.reduce_sum(flnt[:, r:r + 1], fas[r][:], AX.X
                                     ).then_inc(_s_v2, 1)
                        v.wait_ge(_s_v2, v0 + 4 * tb + 2)
                        v.tensor_scalar(flnt[:, 2 + r:3 + r],
                                        flnt[:, r:r + 1],
                                        -1.0 / HID, None, ALU.mult
                                        ).then_inc(_s_v2, 1)
                        v.wait_ge(_s_ev, ev0 + 2 * tb + 2)
                        v.tensor_scalar(fas[r][:], fas[r][:],
                                        flnt[:, 2 + r:3 + r],
                                        flnt[:, 6 + r:7 + r],
                                        ALU.add, ALU.mult).then_inc(_s_v2, 1)
                s_v2.n = v0 + 4 * NQB

                @blk.scalar
                def _(sc):
                    for tb in range(NQB):
                        r = tb % 2
                        sc.wait_ge(_s_v2, v0 + 4 * tb + 3)
                        sc.activation(fdmp[:], fas[r][:], AF.Square,
                                      bias=flnt[:, 2 + r:3 + r],
                                      accum_out=flnt[:, 4 + r:5 + r]
                                      ).then_inc(_s_ev, 1)
                        sc.wait_ge(_s_ev, ev0 + 2 * tb + 1)
                        sc.activation(flnt[:, 6 + r:7 + r],
                                      flnt[:, 4 + r:5 + r], AF.Ln,
                                      bias=epsb[:, 0:1], scale=1.0 / HID)
                        sc.activation(flnt[:, 6 + r:7 + r],
                                      flnt[:, 6 + r:7 + r], AF.Exp,
                                      scale=-0.5).then_inc(_s_ev, 1)
                s_ev.n = ev0 + 2 * NQB

                @blk.gpsimd
                def _(g):
                    for tb in range(NQB):
                        r = tb % 2
                        g.wait_ge(_s_v2, v0 + 4 * tb + 4)
                        g.dma_start(out_loc[tb * 128:(tb + 1) * 128, :],
                                    fas[r][:]).then_inc(_sd_g, 16)
                    g.wait_ge(_sd_g, o0 + 16 * NQB)
                sd_g.n = o0 + 16 * NQB
        return nc
    return nc


def _finish_stub(nc, out_loc, _sd_o, sd_o):
    with nc.Block() as blk:
        @blk.gpsimd
        def _(g):
            with nc.sbuf_tensor("zz", [128, 512], F32) as zz:
                g.memset(zz[:], 0.0)
                for i in range(NQ // 128):
                    for j in range(HID // 512):
                        g.dma_start(out_loc[i * 128:(i + 1) * 128,
                                            j * 512:(j + 1) * 512],
                                    zz[:]).then_inc(_sd_o, 16)
                        sd_o.n += 16
                g.wait_ge(_sd_o, sd_o.n)


def _prep_inputs(inputs):
    x = np.asarray(inputs["L0_attn_hidden_states"], np.float32)
    q_b = np.asarray(inputs["L0_attn_q_b_w"], np.float32)
    kv_a = np.asarray(inputs["L0_attn_kv_a_w"], np.float32)

    perm = np.arange(H * QKD)
    for h in range(H):
        base = h * QKD + NOPE
        perm[base:base + ROPE // 2] = base + np.arange(0, ROPE, 2)
        perm[base + ROPE // 2:base + ROPE] = base + np.arange(1, ROPE, 2)
    q_b_perm = np.ascontiguousarray(q_b[:, perm])

    permk = np.arange(KVR + ROPE)
    permk[KVR:KVR + ROPE // 2] = KVR + np.arange(0, ROPE, 2)
    permk[KVR + ROPE // 2:] = KVR + np.arange(1, ROPE, 2)
    kv_a_perm = np.ascontiguousarray(kv_a[:, permk])

    shared = {
        "q_a_w": np.ascontiguousarray(inputs["L0_attn_q_a_w"], dtype=np.float32),
        "q_a_norm": np.ascontiguousarray(inputs["L0_attn_q_a_norm_w"],
                                         dtype=np.float32),
        "q_b_w": q_b_perm,
        "kv_a_w": kv_a_perm,
        "kv_a_norm": np.ascontiguousarray(inputs["L0_attn_kv_a_norm_w"],
                                          dtype=np.float32),
        "kv_b_w": np.ascontiguousarray(inputs["L0_attn_kv_b_w"], dtype=np.float32),
        "o_w": np.ascontiguousarray(inputs["L0_attn_o_w"], dtype=np.float32),
        "idx_wq": np.ascontiguousarray(inputs["L0_attn_idx_wq_w"],
                                       dtype=np.float32),
        "idx_wk": np.ascontiguousarray(inputs["L0_attn_idx_wk_w"],
                                       dtype=np.float32),
        "idx_ww": np.ascontiguousarray(inputs["L0_attn_idx_weights_w"],
                                       dtype=np.float32),
        "gate_wT": np.ascontiguousarray(
            np.asarray(inputs["L0_ffn_gate_w"], np.int32).T),
        "up_wT": np.ascontiguousarray(
            np.asarray(inputs["L0_ffn_up_w"], np.int32).T),
        "down_wT": np.ascontiguousarray(
            np.asarray(inputs["L0_ffn_down_w"], np.int32).T),
        "gate_s": np.ascontiguousarray(inputs["L0_ffn_gate_s"], dtype=np.float32),
        "up_s": np.ascontiguousarray(inputs["L0_ffn_up_s"], dtype=np.float32),
        "down_s": np.ascontiguousarray(inputs["L0_ffn_down_s"], dtype=np.float32),
    }

    in_maps = []
    for c in range(8):
        b, g = c // 2, c % 2
        blocks = [g + 2 * j for j in range(NQB)]
        rows = np.concatenate([np.arange(gb * 128, (gb + 1) * 128)
                               for gb in blocks])
        qpos_pb = np.empty((128, NQB), np.float32)
        for j, gb in enumerate(blocks):
            qpos_pb[:, j] = gb * 128 + np.arange(128)
        m = dict(shared)
        m["x_full"] = np.ascontiguousarray(x[b])
        m["x_q"] = np.ascontiguousarray(x[b][rows])
        m["qpos_pb"] = qpos_pb
        m["qpos_row"] = np.ascontiguousarray(rows.astype(np.float32)[None, :])
        in_maps.append(m)
    return in_maps


def kernel(**inputs):
    in_maps = _prep_inputs(inputs)
    nc = bass.Bass("TRN2", target_bir_lowering=False)
    build(nc)
    res = run_bass_kernel_spmd(nc, in_maps, core_ids=list(range(8)))
    out = np.empty((B, S, HID), np.float32)
    for c in range(8):
        b, g = c // 2, c % 2
        o = res.results[c]["out_loc"]
        for j in range(NQB):
            gb = g + 2 * j
            out[b, gb * 128:(gb + 1) * 128, :] = o[j * 128:(j + 1) * 128, :]
    return out
